# version 07
# baseline (speedup 1.0000x reference)
import numpy as np
import ml_dtypes
from contextlib import ExitStack

import concourse.bass as bass
import concourse.mybir as mybir
from concourse.bass_utils import run_bass_kernel_spmd

F32 = mybir.dt.float32
BF16 = mybir.dt.bfloat16
AF = mybir.ActivationFunctionType
ALU = mybir.AluOpType
bfloat16 = ml_dtypes.bfloat16

NH = 16
D = 2048
INTER = 8192
SEQ = 2048
TT = 512
NTC = TT // 128
NBLK = TT // 256
NIC = INTER // 128


class Sem:
    __slots__ = ("h", "n")

    def __init__(self, h):
        self.h = h
        self.n = 0


class Prog:
    ENGS = ("sync", "gpsimd", "tensor", "vector", "scalar")

    def __init__(self, nc, ctx):
        self.nc = nc
        self.ctx = ctx
        self.recs = {e: [] for e in self.ENGS}

    def sem(self, name):
        return Sem(self.ctx.enter_context(self.nc.semaphore(name)))

    def op(self, eng, fn, waits=(), incs=()):
        wd = {}
        for s, v in waits:
            if v > 0:
                wd[s] = max(wd.get(s, 0), v)
        for s, n in incs:
            s.n += n
        self.recs[eng].append((tuple(wd.items()), fn, tuple(incs)))

    def emit(self):
        with self.nc.Block() as blk:
            def runner(name):
                recs = self.recs[name]

                def run(e):
                    for waits, fn, incs in recs:
                        for s, v in waits:
                            e.wait_ge(s.h, v)
                        ins = fn(e) if fn is not None else None
                        for s, n in incs:
                            ins.then_inc(s.h, n)
                return run
            blk.sync(runner("sync"))
            blk.gpsimd(runner("gpsimd"))
            blk.tensor(runner("tensor"))
            blk.vector(runner("vector"))
            blk.scalar(runner("scalar"))


class Cfg:
    def __init__(self, n_tiles=8, n_heads=16, debug=False):
        self.n_tiles = n_tiles
        self.n_heads = n_heads
        self.debug = debug


def build_core_program(cfg: Cfg):
    nc = bass.Bass("TRN2", target_bir_lowering=False)
    NHh = cfg.n_heads

    x_d = nc.dram_tensor("x", [2, SEQ, D], F32, kind="ExternalInput").ap()
    wq_d = nc.dram_tensor("wq", [NH, 128, 2048], F32, kind="ExternalInput").ap()
    wk_d = nc.dram_tensor("wk", [NH, 128, 2048], F32, kind="ExternalInput").ap()
    wv_d = nc.dram_tensor("wv", [8, 128, 4096], F32, kind="ExternalInput").ap()
    wg_d = nc.dram_tensor("wg", [8, 128, 4096], F32, kind="ExternalInput").ap()
    wo_d = nc.dram_tensor("wo", [8, 128, 4096], F32, kind="ExternalInput").ap()
    wgu_d = nc.dram_tensor("wgu", [64, 128, 4096], F32, kind="ExternalInput").ap()
    wd_d = nc.dram_tensor("wd", [32, 128, 4096], F32, kind="ExternalInput").ap()
    nw_d = nc.dram_tensor("nw", [128, 16], F32, kind="ExternalInput").ap()
    slope_d = nc.dram_tensor("slope", [1, 16], F32, kind="ExternalInput").ap()
    idxW_d = nc.dram_tensor("idxW", [128, 384], F32, kind="ExternalInput").ap()
    maskW_d = nc.dram_tensor("maskW", [128, 384], F32, kind="ExternalInput").ap()
    fidx_d = nc.dram_tensor("fidx", [128, 256], F32, kind="ExternalInput").ap()
    kdx_d = nc.dram_tensor("kdx", [128, 2], F32, kind="ExternalInput").ap()
    c256_d = nc.dram_tensor("c256", [128, 1], F32, kind="ExternalInput").ap()
    eps5_d = nc.dram_tensor("eps5", [128, 1], F32, kind="ExternalInput").ap()
    eps6_d = nc.dram_tensor("eps6", [128, 1], F32, kind="ExternalInput").ap()
    neghalf_d = nc.dram_tensor("neghalf", [128, 1], F32, kind="ExternalInput").ap()
    inv2048_d = nc.dram_tensor("inv2048", [128, 1], F32, kind="ExternalInput").ap()
    id16_d = nc.dram_tensor("id16", [128, 128], BF16, kind="ExternalInput").ap()
    ones1_d = nc.dram_tensor("ones1", [1, 128], F32, kind="ExternalInput").ap()
    onescol_d = nc.dram_tensor("onescol", [128, 1], BF16, kind="ExternalInput").ap()
    out_d = nc.dram_tensor("out", [2, SEQ, D], F32, kind="ExternalOutput").ap()
    scr_wq = nc.dram_tensor("scr_wq", [NH, 128, 2048], BF16, kind="Internal").ap()
    scr_wk = nc.dram_tensor("scr_wk", [NH, 128, 2048], BF16, kind="Internal").ap()
    scr_wv = nc.dram_tensor("scr_wv", [8, 128, 4096], BF16, kind="Internal").ap()
    scr_wg = nc.dram_tensor("scr_wg", [8, 128, 4096], BF16, kind="Internal").ap()
    scr_wo = nc.dram_tensor("scr_wo", [8, 128, 4096], BF16, kind="Internal").ap()
    scr_wgu = nc.dram_tensor("scr_wgu", [64, 128, 4096], BF16, kind="Internal").ap()
    scr_wd = nc.dram_tensor("scr_wd", [32, 128, 4096], BF16, kind="Internal").ap()

    dbg = {}
    if cfg.debug:
        dbg["xn"] = nc.dram_tensor("dbg_xn", [16, 128, TT], BF16, kind="ExternalOutput").ap()
        dbg["v"] = nc.dram_tensor("dbg_v", [NTC, 128, 2048], BF16, kind="ExternalOutput").ap()
        dbg["qf"] = nc.dram_tensor("dbg_qf", [128, TT], BF16, kind="ExternalOutput").ap()
        dbg["kf"] = nc.dram_tensor("dbg_kf", [128, TT], BF16, kind="ExternalOutput").ap()
        dbg["o"] = nc.dram_tensor("dbg_o", [16, 128, TT], BF16, kind="ExternalOutput").ap()
        dbg["rstd"] = nc.dram_tensor("dbg_rstd", [128, NTC], F32, kind="ExternalOutput").ap()
        dbg["p"] = nc.dram_tensor("dbg_p", [16, 128, TT], BF16, kind="ExternalOutput").ap()
        dbg["s"] = nc.dram_tensor("dbg_s", [NTC, 128, 2048], BF16, kind="ExternalOutput").ap()
        dbg["h"] = nc.dram_tensor("dbg_h", [3, 128, TT], BF16, kind="ExternalOutput").ap()
        dbg["kv"] = nc.dram_tensor("dbg_kv", [128, 2048], F32, kind="ExternalOutput").ap()

    ctx = ExitStack()
    E = ctx.enter_context
    P = Prog(nc, ctx)

    xn_fm = [E(nc.sbuf_tensor(f"xnfm{i}", [128, TT], BF16)) for i in range(16)]
    v_s = [E(nc.sbuf_tensor(f"vs{i}", [128, 2048], BF16)) for i in range(NTC)]
    o_fm = [E(nc.sbuf_tensor(f"ofm{i}", [128, TT], BF16)) for i in range(16)]
    x_st = [E(nc.sbuf_tensor(f"xst{i}", [128, 2048], F32)) for i in range(2)]
    xn_tm = E(nc.sbuf_tensor("xntm", [128, 2048], BF16))
    xp = [E(nc.sbuf_tensor(f"xp{i}", [128, 512], F32)) for i in range(2)]
    op_ = [E(nc.sbuf_tensor(f"opc{i}", [128, 512], F32)) for i in range(2)]
    qf = [E(nc.sbuf_tensor(f"qfb{i}", [128, TT], BF16)) for i in range(2)]
    kf = [E(nc.sbuf_tensor(f"kfb{i}", [128, TT], BF16)) for i in range(2)]
    wq_sl = [E(nc.sbuf_tensor(f"wqsl{i}", [128, 2048], BF16)) for i in range(2)]
    wk_sl = [E(nc.sbuf_tensor(f"wksl{i}", [128, 2048], BF16)) for i in range(2)]
    wc_sl = [E(nc.sbuf_tensor(f"wcsl{i}", [128, 2048], BF16)) for i in range(4)]
    decT = E(nc.sbuf_tensor("decT", [128, 16 * 384], BF16))
    qdecb = E(nc.sbuf_tensor("qdecb", [128, 16 * 256], BF16))
    kv_all = E(nc.sbuf_tensor("kvall", [128, 16 * 128], F32))
    kv_bfa = E(nc.sbuf_tensor("kvbfa", [128, 16 * 128], BF16))
    ksc = [E(nc.sbuf_tensor(f"ksc{i}", [128, 128], BF16)) for i in range(2)]
    ksc2 = [E(nc.sbuf_tensor(f"ksc2{i}", [128, 128], BF16)) for i in range(2)]
    qkd = [E(nc.sbuf_tensor(f"qkd{i}", [128, 256], BF16)) for i in range(2)]
    qsc = E(nc.sbuf_tensor("qsc", [128, 256], BF16))
    o2b = E(nc.sbuf_tensor("o2b", [128, TT], BF16))
    gate_sc = [E(nc.sbuf_tensor(f"gatesc{i}", [128, 512], BF16)) for i in range(2)]
    hsil = [E(nc.sbuf_tensor(f"hsil{i}", [128, 512], BF16)) for i in range(2)]
    g_sc = [E(nc.sbuf_tensor(f"gsc{i}", [128, 512], BF16)) for i in range(2)]
    t1b = [E(nc.sbuf_tensor(f"t1b{i}", [128, 512], F32)) for i in range(2)]
    id16 = E(nc.sbuf_tensor("id16_s", [128, 128], BF16))
    ones1 = E(nc.sbuf_tensor("ones1_s", [1, 128], F32))
    onescol = E(nc.sbuf_tensor("onescol_s", [128, 1], BF16))
    nw_sb = E(nc.sbuf_tensor("nw_s", [128, 16], F32))
    slope_sb = E(nc.sbuf_tensor("slope_s", [1, 16], F32))
    negs = E(nc.sbuf_tensor("negs", [128, 16], F32))
    kdec_all = E(nc.sbuf_tensor("kdecall", [128, 32], F32))
    bd_all = E(nc.sbuf_tensor("bdall", [128, 16], F32))
    sum_c = E(nc.sbuf_tensor("sumc", [128, 1], F32))
    ssq_c = E(nc.sbuf_tensor("ssqc", [128, 1], F32))
    mean_c = E(nc.sbuf_tensor("meanc", [128, 1], F32))
    msq_c = E(nc.sbuf_tensor("msqc", [128, 1], F32))
    var_c = E(nc.sbuf_tensor("varc", [128, 1], F32))
    lnt = E(nc.sbuf_tensor("lnt", [128, 1], F32))
    rstd_c = E(nc.sbuf_tensor("rstdc", [128, 1], F32))
    rstd_all = E(nc.sbuf_tensor("rstdall", [128, NTC], F32))
    eps5 = E(nc.sbuf_tensor("eps5_s", [128, 1], F32))
    eps6 = E(nc.sbuf_tensor("eps6_s", [128, 1], F32))
    neghalf = E(nc.sbuf_tensor("neghalf_s", [128, 1], F32))
    inv2048 = E(nc.sbuf_tensor("inv2048_s", [128, 1], F32))
    tmp_r = E(nc.sbuf_tensor("tmpr", [128, NTC], F32))

    cctx = ExitStack()
    CE = cctx.enter_context
    idxW = CE(nc.sbuf_tensor("idxW_s", [128, 384], F32))
    maskW = CE(nc.sbuf_tensor("maskW_s", [128, 384], F32))
    fidx = CE(nc.sbuf_tensor("fidx_s", [128, 256], F32))
    kdx = CE(nc.sbuf_tensor("kdx_s", [128, 2], F32))
    c256 = CE(nc.sbuf_tensor("c256_s", [128, 1], F32))

    psA = E(nc.psum_tensor("psA", [128, 512], F32))
    psB = E(nc.psum_tensor("psB", [128, 512], F32))
    psC = E(nc.psum_tensor("psC", [128, 512], F32))
    psD = E(nc.psum_tensor("psD", [128, 512], F32))
    psT = E(nc.psum_tensor("psT", [128, 256], BF16))
    psT2 = E(nc.psum_tensor("psT2", [128, 256], BF16))
    psKV = E(nc.psum_tensor("psKV", [128, 128], F32))
    psVar = E(nc.psum_tensor("psVar", [128, NTC], F32))
    banks4 = [psA, psB, psC, psD]

    pe = P.sem("pe")
    act = P.sem("act")
    dve = P.sem("dve")
    s_const = P.sem("sconst")
    sxs = [P.sem(f"sxs{i}") for i in range(2)]
    sxp = [P.sem(f"sxp{i}") for i in range(2)]
    sop = [P.sem(f"sop{i}") for i in range(2)]
    swq = [P.sem(f"swq{i}") for i in range(2)]
    swk = [P.sem(f"swk{i}") for i in range(2)]
    swc = [P.sem(f"swc{i}") for i in range(4)]
    swcH = [P.sem(f"swcH{i}") for i in range(4)]
    swqH = [P.sem(f"swqH{i}") for i in range(2)]
    swkH = [P.sem(f"swkH{i}") for i in range(2)]
    sscrc = [P.sem(f"sscrc{i}") for i in range(4)]
    sscrq = [P.sem(f"sscrq{i}") for i in range(2)]
    sscrk = [P.sem(f"sscrk{i}") for i in range(2)]
    s_dbg = P.sem("sdbg") if cfg.debug else None

    bank_free = {id(b): [] for b in (psA, psB, psC, psD, psT, psT2, psKV, psVar)}
    xstage_free = [[], []]
    xp_free = [[], []]
    op_free = [[], []]
    wq_free = [[], []]
    wk_free = [[], []]
    wc_free = [[], [], [], []]
    gate_free = [[], []]
    ebuf_free = [[], [], [], []]
    t1_free = [[], []]
    xn_tm_free = []
    m_p_done_prevL = []
    m_s_read_prevL = []

    def dma(e, out, in_):
        return e.dma_start(out=out, in_=in_)

    scr_done = {}
    NSL = 4

    def wc_stream(chunk_aps, scr_aps, key, tile):
        st = {"ready": {}, "next": 0, "n": len(chunk_aps)}
        write_scr = (tile == 0)
        from_scr = (tile > 0)
        extra0 = scr_done.get(key, []) if tile == 1 else []

        def load(k, waits):
            sl = k % NSL
            if from_scr:
                P.op("sync", lambda e, k=k, sl=sl: dma(e, wc_sl[sl][:], scr_aps[k]),
                     waits=waits, incs=[(swcH[sl], 16)])
                st["ready"][k] = (swcH[sl], swcH[sl].n)
            else:
                P.op("gpsimd", lambda e, k=k, sl=sl: dma(e, wc_sl[sl][:], chunk_aps[k]),
                     waits=waits, incs=[(swc[sl], 16)])
                st["ready"][k] = (swc[sl], swc[sl].n)
            if write_scr:
                P.op("sync", lambda e, k=k, sl=sl: dma(e, scr_aps[k], wc_sl[sl][:]),
                     waits=[st["ready"][k]], incs=[(sscrc[sl], 16)])
                st[("scr", k)] = sscrc[sl].n

        def start():
            for j in range(min(NSL, st["n"])):
                load(j, wc_free[j] + extra0 +
                     ([(sscrc[j], sscrc[j].n)] if write_scr else []))
            st["next"] = NSL

        def ready(k):
            return [st["ready"][k]]

        def consumed(k, milestone):
            if st["next"] == k + NSL and k + NSL < st["n"]:
                w = [milestone]
                if write_scr:
                    w.append((sscrc[k % NSL], st[("scr", k)]))
                load(k + NSL, w)
                st["next"] = k + NSL + 1
            if k == st["n"] - 1:
                for j in range(NSL):
                    kk = k - j
                    if kk < 0:
                        continue
                    m = [milestone]
                    if write_scr:
                        m.append((sscrc[kk % NSL], st[("scr", kk)]))
                    wc_free[kk % NSL] = m
                if write_scr:
                    scr_done[key] = [(sscrc[j], sscrc[j].n) for j in range(NSL)]
        return start, ready, consumed

    for out_t, in_a in [(idxW, idxW_d), (maskW, maskW_d), (fidx, fidx_d), (kdx, kdx_d),
                        (c256, c256_d), (id16, id16_d), (ones1, ones1_d),
                        (onescol, onescol_d), (nw_sb, nw_d), (slope_sb, slope_d),
                        (eps5, eps5_d), (eps6, eps6_d), (neghalf, neghalf_d),
                        (inv2048, inv2048_d)]:
        P.op("scalar", lambda e, o=out_t, i=in_a: dma(e, o[:], i), incs=[(s_const, 16)])
    CONSTW = [(s_const, s_const.n)]

    P.op("tensor", lambda e: e.matmul(psA[:, 0:16], ones1[:], slope_sb[:], start=True, stop=True),
         waits=CONSTW, incs=[(pe, 1)])
    m = pe.n
    P.op("vector", lambda e: e.tensor_scalar_mul(negs[:], psA[:, 0:16], -1.0),
         waits=CONSTW + [(pe, m)], incs=[(dve, 1)])
    bank_free[id(psA)] = [(dve, dve.n)]
    m_negs = dve.n

    for h in range(NH):
        P.op("vector", lambda e, h=h: e.scalar_tensor_tensor(
            t1b[0][:, 0:384], idxW[:], negs[:, h:h + 1], maskW[:], ALU.mult, ALU.add),
            waits=[(dve, m_negs)] + t1_free[0], incs=[(dve, 1)])
        m1 = dve.n
        P.op("scalar", lambda e, h=h: e.activation(
            decT[:, h * 384:(h + 1) * 384], t1b[0][:, 0:384], AF.Exp),
            waits=[(dve, m1)], incs=[(act, 1)])
        t1_free[0] = [(act, act.n)]
        P.op("vector", lambda e, h=h: e.tensor_scalar_mul(
            t1b[1][:, 0:256], fidx[:], negs[:, h:h + 1]),
            waits=[(dve, m_negs)] + t1_free[1], incs=[(dve, 1)])
        m2 = dve.n
        P.op("scalar", lambda e, h=h: e.activation(
            qdecb[:, h * 256:(h + 1) * 256], t1b[1][:, 0:256], AF.Exp),
            waits=[(dve, m2)], incs=[(act, 1)])
        t1_free[1] = [(act, act.n)]
        P.op("vector", lambda e, h=h: e.tensor_scalar_mul(
            t1b[0][:, 384:386], kdx[:], negs[:, h:h + 1]),
            waits=t1_free[0], incs=[(dve, 1)])
        m3 = dve.n
        P.op("scalar", lambda e, h=h: e.activation(
            kdec_all[:, 2 * h:2 * h + 2], t1b[0][:, 384:386], AF.Exp),
            waits=[(dve, m3)], incs=[(act, 1)])
        t1_free[0] += [(act, act.n)]
        P.op("vector", lambda e, h=h: e.tensor_scalar_mul(
            t1b[1][:, 256:257], c256[:], negs[:, h:h + 1]),
            waits=t1_free[1], incs=[(dve, 1)])
        m4 = dve.n
        P.op("scalar", lambda e, h=h: e.activation(
            bd_all[:, h:h + 1], t1b[1][:, 256:257], AF.Exp),
            waits=[(dve, m4)], incs=[(act, 1)])
        t1_free[1] += [(act, act.n)]
    m_prologue_act = act.n
    t1_free[0] = [(act, m_prologue_act)]
    t1_free[1] = [(act, m_prologue_act)]

    cctx.close()
    h_ex = [E(nc.sbuf_tensor(f"hex{i}", [128, TT], BF16)) for i in range(48)]
    h_t = o_fm + h_ex

    rem = nc.sbuf_bytes_remaining
    assert rem >= 0, f"SBUF over budget: {rem}"
    build_info = {"sbuf_remaining_per_partition": rem}

    for t in range(cfg.n_tiles):
        seq, tis = divmod(t, 4)
        trow0 = tis * TT
        DBG = cfg.debug and t == 0

        if tis == 0:
            P.op("vector", lambda e: e.memset(kv_all[:], 0.0),
                 waits=[(pe, pe.n)], incs=[(dve, 1)])
            P.op("vector", lambda e: e.memset(kv_bfa[:], 0.0), incs=[(dve, 1)])
        m_kvbf_carry = dve.n

        for tc in range(NTC):
            sl = tc % 2
            P.op("gpsimd", lambda e, seq=seq, r=trow0 + tc * 128, sl=sl:
                 dma(e, x_st[sl][:], x_d[seq, r:r + 128, :]),
                 waits=xstage_free[sl], incs=[(sxs[sl], 16)])
            mx = sxs[sl].n
            P.op("scalar", lambda e, sl=sl: e.activation(
                xn_tm[:], x_st[sl][:], AF.Identity, accum_out=sum_c[:]),
                waits=[(sxs[sl], mx)] + xn_tm_free, incs=[(act, 1)])
            P.op("scalar", lambda e, sl=sl: e.activation(
                xn_tm[:], x_st[sl][:], AF.Square, accum_out=ssq_c[:]),
                waits=[(act, act.n)], incs=[(act, 1)])
            m_ss = act.n
            P.op("vector", lambda e: e.tensor_scalar_mul(mean_c[:], sum_c[:], 1.0 / 2048.0),
                 waits=[(act, m_ss)], incs=[(dve, 1)])
            P.op("vector", lambda e: e.tensor_tensor(msq_c[:], mean_c[:], mean_c[:], ALU.mult),
                 waits=[(dve, dve.n)], incs=[(dve, 1)])
            P.op("vector", lambda e: e.tensor_scalar_mul(var_c[:], ssq_c[:], 1.0 / 2048.0),
                 waits=[(dve, dve.n)], incs=[(dve, 1)])
            P.op("vector", lambda e: e.tensor_tensor(var_c[:], var_c[:], msq_c[:], ALU.subtract),
                 waits=[(dve, dve.n)], incs=[(dve, 1)])
            m_mv = dve.n
            P.op("scalar", lambda e: e.activation(lnt[:], var_c[:], AF.Ln, bias=eps5[:]),
                 waits=[(dve, m_mv)], incs=[(act, 1)])
            P.op("scalar", lambda e: e.activation(rstd_c[:], lnt[:], AF.Exp, scale=neghalf[:]),
                 waits=[(act, act.n)], incs=[(act, 1)])
            m_rc = act.n
            P.op("vector", lambda e, sl=sl: e.tensor_scalar(
                xn_tm[:], x_st[sl][:], mean_c[:], rstd_c[:], ALU.subtract, ALU.mult),
                waits=[(act, m_rc), (dve, dve.n)], incs=[(dve, 1)])
            m_xntm_d = dve.n
            xstage_free[sl] = [(dve, m_xntm_d), (act, m_ss)]
            for g in range(8):
                bank = psT if g % 2 == 0 else psT2
                dc0 = 2 * g
                P.op("tensor", lambda e, dc0=dc0, bank=bank: e.transpose(
                    bank[:, 0:128], xn_tm[:, dc0 * 128:(dc0 + 1) * 128], id16[:]),
                    waits=[(dve, m_xntm_d)] + bank_free[id(bank)])
                P.op("tensor", lambda e, dc0=dc0, bank=bank: e.transpose(
                    bank[:, 128:256], xn_tm[:, (dc0 + 1) * 128:(dc0 + 2) * 128], id16[:]),
                    incs=[(pe, 1)])
                m_tr = pe.n
                P.op("vector", lambda e, dc0=dc0, tc=tc, bank=bank: e.tensor_copy(
                    xn_fm[dc0][:, tc * 128:(tc + 1) * 128], bank[:, 0:128]),
                    waits=[(pe, m_tr)], incs=[(dve, 1)])
                P.op("vector", lambda e, dc0=dc0, tc=tc, bank=bank: e.tensor_copy(
                    xn_fm[dc0 + 1][:, tc * 128:(tc + 1) * 128], bank[:, 128:256]),
                    incs=[(dve, 1)])
                bank_free[id(bank)] = [(dve, dve.n)]
            xn_tm_free = [(pe, pe.n)]
        m_xn = dve.n
        if DBG:
            for dc in range(16):
                P.op("sync", lambda e, dc=dc: dma(e, dbg["xn"][dc], xn_fm[dc][:]),
                     waits=[(dve, m_xn)], incs=[(s_dbg, 16)])

        mpj = {}

        def emit_wqk_load(h):
            sl = h % 2
            if t == 0:
                P.op("gpsimd", lambda e, h=h, sl=sl: dma(e, wq_sl[sl][:], wq_d[h]),
                     waits=wq_free[sl] + [(sscrq[sl], sscrq[sl].n)], incs=[(swq[sl], 16)])
                mwq = swq[sl].n
                P.op("sync", lambda e, h=h, sl=sl: dma(e, scr_wq[h], wq_sl[sl][:]),
                     waits=[(swq[sl], mwq)], incs=[(sscrq[sl], 16)])
                P.op("gpsimd", lambda e, h=h, sl=sl: dma(e, wk_sl[sl][:], wk_d[h]),
                     waits=wk_free[sl] + [(sscrk[sl], sscrk[sl].n)], incs=[(swk[sl], 16)])
                mwk = swk[sl].n
                P.op("sync", lambda e, h=h, sl=sl: dma(e, scr_wk[h], wk_sl[sl][:]),
                     waits=[(swk[sl], mwk)], incs=[(sscrk[sl], 16)])
                if h == NHh - 2:
                    scr_done["wq0"] = [(sscrq[0], sscrq[0].n), (sscrk[0], sscrk[0].n)]
                if h == NHh - 1:
                    scr_done["wq1"] = [(sscrq[1], sscrq[1].n), (sscrk[1], sscrk[1].n)]
                mpj[("w", h)] = ((swq[sl], mwq), (swk[sl], mwk))
                return
            extra = scr_done.get("wq" + str(sl), []) if t == 1 else []
            P.op("sync", lambda e, h=h, sl=sl: dma(e, wq_sl[sl][:], scr_wq[h]),
                 waits=wq_free[sl] + extra, incs=[(swqH[sl], 16)])
            mwq = swqH[sl].n
            P.op("sync", lambda e, h=h, sl=sl: dma(e, wk_sl[sl][:], scr_wk[h]),
                 waits=wk_free[sl] + extra, incs=[(swkH[sl], 16)])
            mwk = swkH[sl].n
            mpj[("w", h)] = ((swqH[sl], mwq), (swkH[sl], mwk))

        emit_wqk_load(0)
        if NHh > 1:
            emit_wqk_load(1)

        v_start, v_ready, v_consumed = wc_stream(
            [wv_d[c // 2, :, (c % 2) * 2048:(c % 2 + 1) * 2048] for c in range(16)],
            [scr_wv[c // 2, :, (c % 2) * 2048:(c % 2 + 1) * 2048] for c in range(16)],
            "wv", t)
        v_start()
        for vt in range(4):
            for dc in range(16):
                ck = 4 * vt + dc // 4
                sl = ck % 4
                for tc in range(NTC):
                    w = []
                    if dc == 0 and tc == 0:
                        w += [(dve, m_xn)]
                        for bk in banks4:
                            w += bank_free[id(bk)]
                    if dc % 4 == 0 and tc == 0:
                        w += v_ready(ck)
                    P.op("tensor", lambda e, dc=dc, tc=tc, sl=sl: e.matmul(
                        banks4[tc][:], xn_fm[dc][:, tc * 128:(tc + 1) * 128],
                        wc_sl[sl][:, (dc % 4) * 512:(dc % 4 + 1) * 512],
                        start=(dc == 0), stop=(dc == 15)),
                        waits=w,
                        incs=([(pe, 1)] if (tc == NTC - 1 and dc % 4 == 3) else ()))
                if dc % 4 == 3:
                    v_consumed(ck, (pe, pe.n))
            m_vg = pe.n
            for tc in range(NTC):
                P.op("scalar", lambda e, tc=tc, vt=vt: e.activation(
                    v_s[tc][:, vt * 512:(vt + 1) * 512], banks4[tc][:], AF.Silu),
                    waits=[(pe, m_vg)] + m_s_read_prevL, incs=[(act, 1)])
                bank_free[id(banks4[tc])] = [(act, act.n)]
        m_v_done = act.n
        if DBG:
            for tc in range(NTC):
                P.op("sync", lambda e, tc=tc: dma(e, dbg["v"][tc], v_s[tc][:]),
                     waits=[(act, m_v_done)], incs=[(s_dbg, 16)])

        m_kvbf = m_kvbf_carry
        mqk = {}

        def emit_proj(h, which):
            sl = h % 2
            wsl, dst, bank = ((wq_sl, qf, psA), (wk_sl, kf, psB))[which]
            mval = mpj[("w", h)][which]
            for dc in range(16):
                w = ([mval, (dve, m_xn)] + bank_free[id(bank)]) if dc == 0 else []
                P.op("tensor", lambda e, dc=dc, sl=sl, wsl=wsl, bank=bank: e.matmul(
                    bank[:], wsl[sl][:, dc * 128:(dc + 1) * 128], xn_fm[dc][:, 0:TT],
                    start=(dc == 0), stop=(dc == 15)),
                    waits=w, incs=([(pe, 1)] if dc == 15 else ()))
            m_pj = pe.n
            P.op("scalar", lambda e, sl=sl, dst=dst, bank=bank: e.activation(
                dst[sl][:], bank[:], AF.Silu),
                waits=[(pe, m_pj)], incs=[(act, 1)])
            bank_free[id(bank)] = [(act, act.n)]
            if which == 0:
                wq_free[sl] = [(pe, m_pj)]
            else:
                wk_free[sl] = [(pe, m_pj)]
                mqk[h] = act.n

        emit_proj(0, 0)
        emit_proj(0, 1)

        pe_var_prev = 0
        for h in range(NHh):
            sl = h % 2
            if h + 2 < NHh:
                emit_wqk_load(h + 2)
            hs = h * 128
            P.op("tensor", lambda e, sl=sl: e.transpose(
                psT[:, 0:128], kf[sl][:, 0:128], id16[:]),
                waits=[(act, mqk[h])] + bank_free[id(psT)])
            P.op("tensor", lambda e, sl=sl: e.transpose(
                psT[:, 128:256], kf[sl][:, 128:256], id16[:]), incs=[(pe, 1)])
            m_tr0 = pe.n
            for nt in range(2):
                P.op("vector", lambda e, nt=nt, h=h: e.tensor_scalar_mul(
                    ksc[nt][:], psT[:, nt * 128:(nt + 1) * 128],
                    kdec_all[:, 2 * h + nt:2 * h + nt + 1]),
                    waits=[(pe, m_tr0)], incs=[(dve, 1)])
            bank_free[id(psT)] = [(dve, dve.n)]
            m_ksc0 = dve.n
            P.op("tensor", lambda e, sl=sl: e.matmul(
                psC[:, 0:256], kf[sl][:, 0:128], qf[sl][:, 0:256],
                start=True, stop=True), waits=bank_free[id(psC)])
            P.op("tensor", lambda e, sl=sl: e.matmul(
                psC[:, 256:512], kf[sl][:, 128:256], qf[sl][:, 0:256],
                start=True, stop=True), incs=[(pe, 1)])
            m_qkT0 = pe.n
            for nt in range(2):
                off = h * 384 + (128 if nt == 0 else 0)
                P.op("vector", lambda e, nt=nt, off=off: e.tensor_tensor(
                    qkd[nt][:], psC[:, nt * 256:(nt + 1) * 256],
                    decT[:, off:off + 256], ALU.mult),
                    waits=[(pe, m_qkT0)], incs=[(dve, 1)])
            bank_free[id(psC)] = [(dve, dve.n)]
            P.op("vector", lambda e, sl=sl, h=h: e.tensor_tensor(
                qsc[:], qf[sl][:, 0:256], qdecb[:, h * 256:h * 256 + 256], ALU.mult),
                incs=[(dve, 1)])
            m_qkd0 = dve.n
            if h + 1 < NHh:
                emit_proj(h + 1, 0)
            P.op("tensor", lambda e, hs=hs: e.matmul(
                psD[:, 0:256], kv_bfa[:, hs:hs + 128], qsc[:], start=True, stop=False),
                waits=[(dve, m_qkd0), (dve, m_kvbf), (act, m_v_done)] + bank_free[id(psD)])
            P.op("tensor", lambda e, hs=hs: e.matmul(
                psD[:, 0:256], v_s[0][:, hs:hs + 128], qkd[0][:], start=False, stop=False))
            P.op("tensor", lambda e, hs=hs: e.matmul(
                psD[:, 0:256], v_s[1][:, hs:hs + 128], qkd[1][:], start=False, stop=True),
                incs=[(pe, 1)])
            m_o0 = pe.n
            P.op("tensor", lambda e, hs=hs: e.matmul(
                psKV[:], ksc[0][:], v_s[0][:, hs:hs + 128], start=True, stop=False),
                waits=[(dve, m_ksc0), (act, m_v_done)] + bank_free[id(psKV)])
            P.op("tensor", lambda e, hs=hs: e.matmul(
                psKV[:], ksc[1][:], v_s[1][:, hs:hs + 128], start=False, stop=True),
                incs=[(pe, 1)])
            m_kv0 = pe.n
            P.op("scalar", lambda e, hs=hs: e.activation(
                o_fm[hs // 128][:, 0:256], psD[:, 0:256], AF.Copy),
                waits=[(pe, m_o0)] + m_p_done_prevL, incs=[(act, 1)])
            bank_free[id(psD)] = [(act, act.n)]
            P.op("vector", lambda e, hs=hs: e.scalar_tensor_tensor(
                kv_all[:, hs:hs + 128], kv_all[:, hs:hs + 128],
                bd_all[:, hs // 128:hs // 128 + 1], psKV[:], ALU.mult, ALU.add),
                waits=[(pe, m_kv0)], incs=[(dve, 1)])
            bank_free[id(psKV)] = [(dve, dve.n)]
            P.op("vector", lambda e, hs=hs: e.tensor_copy(
                kv_bfa[:, hs:hs + 128], kv_all[:, hs:hs + 128]),
                waits=[(dve, dve.n)], incs=[(dve, 1)])
            m_kvbf0 = dve.n
            P.op("tensor", lambda e, sl=sl: e.transpose(
                psT2[:, 0:128], kf[sl][:, 256:384], id16[:]),
                waits=bank_free[id(psT2)])
            P.op("tensor", lambda e, sl=sl: e.transpose(
                psT2[:, 128:256], kf[sl][:, 384:512], id16[:]), incs=[(pe, 1)])
            m_tr1 = pe.n
            for nt in range(2):
                P.op("vector", lambda e, nt=nt, h=h: e.tensor_scalar_mul(
                    ksc2[nt][:], psT2[:, nt * 128:(nt + 1) * 128],
                    kdec_all[:, 2 * h + nt:2 * h + nt + 1]),
                    waits=[(pe, m_tr1)], incs=[(dve, 1)])
            bank_free[id(psT2)] = [(dve, dve.n)]
            m_ksc1 = dve.n
            P.op("tensor", lambda e, sl=sl: e.matmul(
                psC[:, 0:256], kf[sl][:, 256:384], qf[sl][:, 256:512],
                start=True, stop=True), waits=bank_free[id(psC)])
            P.op("tensor", lambda e, sl=sl: e.matmul(
                psC[:, 256:512], kf[sl][:, 384:512], qf[sl][:, 256:512],
                start=True, stop=True), incs=[(pe, 1)])
            m_qkT1 = pe.n
            for nt in range(2):
                off = h * 384 + (128 if nt == 0 else 0)
                P.op("vector", lambda e, nt=nt, off=off: e.tensor_tensor(
                    qkd[nt][:], psC[:, nt * 256:(nt + 1) * 256],
                    decT[:, off:off + 256], ALU.mult),
                    waits=[(pe, m_qkT1)], incs=[(dve, 1)])
            bank_free[id(psC)] = [(dve, dve.n)]
            P.op("vector", lambda e, sl=sl, h=h: e.tensor_tensor(
                qsc[:], qf[sl][:, 256:512], qdecb[:, h * 256:h * 256 + 256], ALU.mult),
                waits=[(pe, m_qkT1)], incs=[(dve, 1)])
            m_qkd1 = dve.n
            if h + 1 < NHh:
                emit_proj(h + 1, 1)
            P.op("tensor", lambda e, hs=hs: e.matmul(
                psD[:, 0:256], kv_bfa[:, hs:hs + 128], qsc[:], start=True, stop=False),
                waits=[(dve, m_qkd1), (dve, m_kvbf0)] + bank_free[id(psD)])
            P.op("tensor", lambda e, hs=hs: e.matmul(
                psD[:, 0:256], v_s[2][:, hs:hs + 128], qkd[0][:], start=False, stop=False))
            P.op("tensor", lambda e, hs=hs: e.matmul(
                psD[:, 0:256], v_s[3][:, hs:hs + 128], qkd[1][:], start=False, stop=True),
                incs=[(pe, 1)])
            m_o1 = pe.n
            P.op("tensor", lambda e, hs=hs: e.matmul(
                psKV[:], ksc2[0][:], v_s[2][:, hs:hs + 128], start=True, stop=False),
                waits=[(dve, m_ksc1)] + bank_free[id(psKV)])
            P.op("tensor", lambda e, hs=hs: e.matmul(
                psKV[:], ksc2[1][:], v_s[3][:, hs:hs + 128], start=False, stop=True),
                incs=[(pe, 1)])
            m_kv1 = pe.n
            P.op("scalar", lambda e, hs=hs: e.activation(
                o_fm[hs // 128][:, 256:512], psD[:, 0:256], AF.Copy),
                waits=[(pe, m_o1)] + m_p_done_prevL, incs=[(act, 1)])
            bank_free[id(psD)] = [(act, act.n)]
            m_oev = act.n
            P.op("vector", lambda e, hs=hs: e.scalar_tensor_tensor(
                kv_all[:, hs:hs + 128], kv_all[:, hs:hs + 128],
                bd_all[:, hs // 128:hs // 128 + 1], psKV[:], ALU.mult, ALU.add),
                waits=[(pe, m_kv1)], incs=[(dve, 1)])
            bank_free[id(psKV)] = [(dve, dve.n)]
            P.op("vector", lambda e, hs=hs: e.tensor_copy(
                kv_bfa[:, hs:hs + 128], kv_all[:, hs:hs + 128]),
                waits=[(dve, dve.n)], incs=[(dve, 1)])
            m_kvbf = dve.n
            P.op("vector", lambda e, h=h: e.tensor_tensor(
                o2b[:], o_fm[h][:], o_fm[h][:], ALU.mult),
                waits=[(act, m_oev), (pe, pe_var_prev)], incs=[(dve, 1)])
            m_o2 = dve.n
            for tc in range(NTC):
                first = (h == 0 and tc == 0)
                last = (h == NHh - 1 and tc == NTC - 1)
                w = ([(dve, m_o2)] + bank_free[id(psVar)]) if tc == 0 else []
                P.op("tensor", lambda e, tc=tc, first=first, last=last: e.matmul(
                    psVar[:, tc:tc + 1], o2b[:, tc * 128:(tc + 1) * 128], onescol[:],
                    start=first, stop=last),
                    waits=w, incs=([(pe, 1)] if tc == NTC - 1 else ()))
            pe_var_prev = pe.n
        m_var = pe.n
        if DBG:
            for hh in range(16):
                P.op("sync", lambda e, hh=hh: dma(e, dbg["o"][hh], o_fm[hh][:]),
                     waits=[(act, m_oev)], incs=[(s_dbg, 16)])

        P.op("scalar", lambda e: e.activation(
            tmp_r[:], psVar[:], AF.Ln, bias=eps6[:]),
            waits=[(pe, m_var)], incs=[(act, 1)])
        P.op("scalar", lambda e: e.activation(rstd_all[:], tmp_r[:], AF.Exp, scale=neghalf[:]),
             waits=[(act, act.n)], incs=[(act, 1)])
        m_rstd = act.n
        bank_free[id(psVar)] = [(act, m_rstd)]
        if DBG:
            P.op("sync", lambda e: dma(e, dbg["rstd"], rstd_all[:]),
                 waits=[(act, m_rstd)], incs=[(s_dbg, 16)])
            P.op("sync", lambda e: dma(e, dbg["kv"], kv_all[:]),
                 waits=[(dve, m_kvbf)], incs=[(s_dbg, 16)])

        g_start, g_ready, g_consumed = wc_stream(
            [wg_d[c // 2, :, (c % 2) * 2048:(c % 2 + 1) * 2048] for c in range(16)],
            [scr_wg[c // 2, :, (c % 2) * 2048:(c % 2 + 1) * 2048] for c in range(16)],
            "wg", t)
        g_start()
        for ocq in range(4):
            for dc in range(16):
                ck = 4 * ocq + dc // 4
                sl = ck % 4
                for oc4 in range(4):
                    w = []
                    if dc == 0 and oc4 == 0:
                        for bk in banks4:
                            w += bank_free[id(bk)]
                    if dc % 4 == 0 and oc4 == 0:
                        w += g_ready(ck)
                    P.op("tensor", lambda e, dc=dc, sl=sl, oc4=oc4: e.matmul(
                        banks4[oc4][:],
                        wc_sl[sl][:, (dc % 4) * 512 + oc4 * 128:(dc % 4) * 512 + (oc4 + 1) * 128],
                        xn_fm[dc][:, 0:TT], start=(dc == 0), stop=(dc == 15)),
                        waits=w,
                        incs=([(pe, 1)] if (oc4 == 3 and dc % 4 == 3) else ()))
                if dc % 4 == 3:
                    g_consumed(ck, (pe, pe.n))
            gbufs = [gate_sc[0], gate_sc[1], hsil[0], hsil[1]]
            for oc4 in range(4):
                oc = ocq * 4 + oc4
                P.op("scalar", lambda e, oc4=oc4, gbufs=gbufs: e.activation(
                    gbufs[oc4][:], banks4[oc4][:], AF.Sigmoid),
                    waits=[(pe, pe.n)] + (gate_free[oc4 % 2] if oc4 < 2 else
                                          ebuf_free[2 + oc4 % 2]), incs=[(act, 1)])
                m_gate = act.n
                bank_free[id(banks4[oc4])] = [(act, m_gate)]
                P.op("vector", lambda e, oc=oc, oc4=oc4, gbufs=gbufs: e.scalar_tensor_tensor(
                    o_fm[oc][:], o_fm[oc][:], nw_sb[:, oc:oc + 1], gbufs[oc4][:],
                    ALU.mult, ALU.mult),
                    waits=[(act, m_gate)], incs=[(dve, 1)])
                if oc4 < 2:
                    gate_free[oc4 % 2] = [(dve, dve.n)]
                else:
                    ebuf_free[2 + oc4 % 2] = [(dve, dve.n)]
        m_p = dve.n
        m_p_done_prevL = [(dve, m_p)]
        if DBG:
            for hh in range(16):
                P.op("sync", lambda e, hh=hh: dma(e, dbg["p"][hh], o_fm[hh][:]),
                     waits=[(dve, m_p)], incs=[(s_dbg, 16)])

        s_start, s_ready, s_consumed = wc_stream(
            [wo_d[c // 2, :, (c % 2) * 2048:(c % 2 + 1) * 2048] for c in range(16)],
            [scr_wo[c // 2, :, (c % 2) * 2048:(c % 2 + 1) * 2048] for c in range(16)],
            "wo", t)
        s_start()
        for dt in range(4):
            for oc in range(16):
                ck = 4 * dt + oc // 4
                sl = ck % 4
                for tc in range(NTC):
                    w = []
                    if oc == 0 and tc == 0:
                        w += [(dve, m_p)]
                        for bk in banks4:
                            w += bank_free[id(bk)]
                    if oc % 4 == 0 and tc == 0:
                        w += s_ready(ck)
                    P.op("tensor", lambda e, oc=oc, tc=tc, sl=sl: e.matmul(
                        banks4[tc][:], o_fm[oc][:, tc * 128:(tc + 1) * 128],
                        wc_sl[sl][:, (oc % 4) * 512:(oc % 4 + 1) * 512],
                        start=(oc == 0), stop=(oc == 15)),
                        waits=w,
                        incs=([(pe, 1)] if (tc == NTC - 1 and oc % 4 == 3) else ()))
                if oc % 4 == 3:
                    s_consumed(ck, (pe, pe.n))
            m_sg = pe.n
            for tc in range(NTC):
                P.op("scalar", lambda e, tc=tc, dt=dt: e.mul(
                    v_s[tc][:, dt * 512:(dt + 1) * 512], banks4[tc][:], rstd_all[:, tc:tc + 1]),
                    waits=[(pe, m_sg)], incs=[(act, 1)])
                bank_free[id(banks4[tc])] = [(act, act.n)]
        m_s_done = act.n
        m_s_pe = pe.n
        if DBG:
            for tc in range(NTC):
                P.op("sync", lambda e, tc=tc: dma(e, dbg["s"][tc], v_s[tc][:]),
                     waits=[(act, m_s_done)], incs=[(s_dbg, 16)])

        u_start, u_ready, u_consumed = wc_stream(
            [wgu_d[c // 2, :, (c % 2) * 2048:(c % 2 + 1) * 2048] for c in range(128)],
            [scr_wgu[c // 2, :, (c % 2) * 2048:(c % 2 + 1) * 2048] for c in range(128)],
            "wgu", t)
        u_start()
        for ic in range(NIC):
            pair = (psA, psB) if ic % 2 == 0 else (psC, psD)
            for gu in range(2):
                bank = pair[gu]
                ck = 2 * ic + gu
                sl = ck % 4
                for dc in range(16):
                    w = []
                    if dc == 0:
                        w += u_ready(ck) + bank_free[id(bank)]
                    P.op("tensor", lambda e, dc=dc, sl=sl, bank=bank: e.matmul(
                        bank[:], wc_sl[sl][:, dc * 128:(dc + 1) * 128],
                        xn_fm[dc][:, 0:TT], start=(dc == 0), stop=(dc == 15)),
                        waits=w, incs=([(pe, 1)] if dc == 15 else ()))
                u_consumed(ck, (pe, pe.n))
            m_gu = pe.n
            hsl = ic % 2
            P.op("scalar", lambda e, hsl=hsl, pair=pair: e.activation(
                hsil[hsl][:], pair[0][:], AF.Silu),
                waits=[(pe, m_gu)] + ebuf_free[2 + hsl], incs=[(act, 1)])
            m_hs = act.n
            bank_free[id(pair[0])] = [(act, m_hs)]
            extra = [(pe, m_s_pe)] if ic < 16 else []
            P.op("vector", lambda e, ic=ic, hsl=hsl, pair=pair: e.tensor_tensor(
                h_t[ic][:], hsil[hsl][:], pair[1][:], ALU.mult),
                waits=[(act, m_hs)] + extra, incs=[(dve, 1)])
            bank_free[id(pair[1])] = [(dve, dve.n)]
            ebuf_free[2 + hsl] = [(dve, dve.n)]
        m_h = dve.n
        if DBG:
            for j, ic in enumerate((0, 16, 63)):
                P.op("sync", lambda e, j=j, ic=ic: dma(e, dbg["h"][j], h_t[ic][:]),
                     waits=[(dve, m_h)], incs=[(s_dbg, 16)])

        d_start, d_ready, d_consumed = wc_stream(
            [wd_d[c // 2, :, (c % 2) * 2048:(c % 2 + 1) * 2048] for c in range(64)],
            [scr_wd[c // 2, :, (c % 2) * 2048:(c % 2 + 1) * 2048] for c in range(64)],
            "wd", t)
        d_start()
        pieces = [(dt, tc) for dt in range(4) for tc in range(NTC)]
        piece_ready = {}
        for k in range(2):
            psl = k % 2
            dt, tc = pieces[k]
            P.op("gpsimd", lambda e, seq=seq, r=trow0 + tc * 128, dt=dt, psl=psl:
                 dma(e, xp[psl][:], x_d[seq, r:r + 128, dt * 512:(dt + 1) * 512]),
                 waits=xp_free[psl], incs=[(sxp[psl], 16)])
            piece_ready[k] = sxp[psl].n

        for dt in range(4):
            for ic in range(NIC):
                ck = dt * 16 + ic // 4
                sl = ck % 4
                for tc in range(NTC):
                    w = []
                    if ic % 4 == 0 and tc == 0:
                        w += d_ready(ck)
                    if ic == 0 and tc == 0:
                        w += [(dve, m_h)]
                        for bk in banks4:
                            w += bank_free[id(bk)]
                    P.op("tensor", lambda e, ic=ic, tc=tc, sl=sl: e.matmul(
                        banks4[tc][:], h_t[ic][:, tc * 128:(tc + 1) * 128],
                        wc_sl[sl][:, (ic % 4) * 512:(ic % 4 + 1) * 512],
                        start=(ic == 0), stop=(ic == NIC - 1)),
                        waits=w,
                        incs=([(pe, 1)] if (tc == NTC - 1 and ic % 4 == 3) else ()))
                if ic % 4 == 3:
                    d_consumed(ck, (pe, pe.n))
            m_dn = pe.n
            ebufs = [g_sc[0], g_sc[1], hsil[0], hsil[1]]
            for tc in range(NTC):
                k = dt * NTC + tc
                P.op("scalar", lambda e, tc=tc, ebufs=ebufs: e.activation(
                    ebufs[tc][:], banks4[tc][:], AF.Silu),
                    waits=[(pe, m_dn)] + ebuf_free[tc], incs=[(act, 1)])
                m_gg = act.n
                bank_free[id(banks4[tc])] = [(act, m_gg)]
                tsl = tc % 2
                P.op("vector", lambda e, tc=tc, dt=dt, tsl=tsl, ebufs=ebufs: e.tensor_tensor(
                    t1b[tsl][:], v_s[tc][:, dt * 512:(dt + 1) * 512], ebufs[tc][:], ALU.mult),
                    waits=[(act, m_gg)] + t1_free[tsl], incs=[(dve, 1)])
                ebuf_free[tc] = [(dve, dve.n)]
                psl = k % 2
                osl = k % 2
                P.op("vector", lambda e, tsl=tsl, psl=psl, osl=osl: e.tensor_tensor(
                    op_[osl][:], t1b[tsl][:], xp[psl][:], ALU.add),
                    waits=[(sxp[psl], piece_ready[k]), (dve, dve.n)] + op_free[osl],
                    incs=[(dve, 1)])
                m_out = dve.n
                t1_free[tsl] = [(dve, m_out)]
                xp_free[psl] = [(dve, m_out)]
                P.op("gpsimd", lambda e, seq=seq, r=trow0 + tc * 128, dt=dt, osl=osl:
                     dma(e, out_d[seq, r:r + 128, dt * 512:(dt + 1) * 512], op_[osl][:]),
                     waits=[(dve, m_out)], incs=[(sop[osl], 16)])
                op_free[osl] = [(sop[osl], sop[osl].n)]
                if k + 2 < 16:
                    dt2, tc2 = pieces[k + 2]
                    psl2 = (k + 2) % 2
                    P.op("gpsimd", lambda e, seq=seq, r=trow0 + tc2 * 128, dt2=dt2, psl2=psl2:
                         dma(e, xp[psl2][:], x_d[seq, r:r + 128, dt2 * 512:(dt2 + 1) * 512]),
                         waits=xp_free[psl2], incs=[(sxp[psl2], 16)])
                    piece_ready[k + 2] = sxp[psl2].n

        m_s_read_prevL = [(dve, dve.n)]

    fin = [(sop[0], sop[0].n), (sop[1], sop[1].n)]
    if cfg.debug:
        fin.append((s_dbg, s_dbg.n))
    P.op("gpsimd", None, waits=fin)

    build_info["recs"] = {k: len(v) for k, v in P.recs.items()}
    P.emit()
    ctx.close()
    return nc, build_info


def host_prep(x_full, slope_rate, w_qkv, w_gate, w_out, norm_w, w_gate_up, w_down,
              n_cores=8):
    f = np.float32
    x_full = np.ascontiguousarray(np.asarray(x_full, dtype=f))
    w_qkv = np.asarray(w_qkv, dtype=f)
    w_gate = np.asarray(w_gate, dtype=f)
    w_out = np.asarray(w_out, dtype=f)
    w_gate_up = np.asarray(w_gate_up, dtype=f)
    w_down = np.asarray(w_down, dtype=f)
    norm_w = np.asarray(norm_w, dtype=f)
    slope_rate = np.asarray(slope_rate, dtype=f)

    qkvw = w_qkv.reshape(NH, 3, 128, D)
    q_w, k_w, v_w3 = qkvw[:, 0], qkvw[:, 1], qkvw[:, 2]
    wq = np.ascontiguousarray(
        q_w.reshape(NH, 128, 16, 128).transpose(0, 3, 2, 1).reshape(NH, 128, 2048))
    wk = np.ascontiguousarray(
        k_w.reshape(NH, 128, 16, 128).transpose(0, 3, 2, 1).reshape(NH, 128, 2048))
    v_w = v_w3.reshape(2048, 2048)
    wv = np.ascontiguousarray(
        v_w.reshape(4, 512, 2, 8, 128).transpose(0, 2, 4, 3, 1).reshape(8, 128, 4096))
    wg = np.ascontiguousarray(
        w_gate.reshape(4, 4, 128, 2, 8, 128).transpose(0, 3, 5, 4, 1, 2).reshape(8, 128, 4096))
    wo = np.ascontiguousarray(
        w_out.reshape(4, 512, 2, 8, 128).transpose(0, 2, 4, 3, 1).reshape(8, 128, 4096))
    wgu = np.ascontiguousarray(
        w_gate_up.reshape(2, 64, 128, 16, 128).transpose(1, 4, 0, 3, 2).reshape(64, 128, 4096))
    wd = np.ascontiguousarray(
        w_down.reshape(4, 512, 8, 8, 128).transpose(0, 2, 4, 3, 1).reshape(32, 128, 4096))
    nw = np.ascontiguousarray(norm_w.reshape(16, 128).T)
    slope = np.ascontiguousarray(slope_rate.reshape(1, 16))

    p = np.arange(128, dtype=f)[:, None]
    j = np.arange(384, dtype=f)[None, :]
    idxW = np.ascontiguousarray(j - 128.0 - p)
    maskW = np.where(idxW >= 0, f(0.0), f(-30000.0)).astype(f)
    ff = np.arange(256, dtype=f)[None, :] + 1.0
    fidx = np.ascontiguousarray(np.broadcast_to(ff, (128, 256)).astype(f))
    nt = np.arange(2, dtype=f)[None, :]
    kdx = np.ascontiguousarray(255.0 - (nt * 128.0 + p)).astype(f)
    c256 = np.full((128, 1), 256.0, dtype=f)
    eps5 = np.full((128, 1), 1e-5, dtype=f)
    eps6 = np.full((128, 1), 1e-6, dtype=f)
    neghalf = np.full((128, 1), -0.5, dtype=f)
    inv2048 = np.full((128, 1), 1.0 / 2048.0, dtype=f)
    id16 = np.eye(128, dtype=bfloat16)
    ones1 = np.ones((1, 128), dtype=f)
    onescol = np.full((128, 1), 1.0 / 2048.0, dtype=bfloat16)

    common = dict(wq=wq, wk=wk, wv=wv, wg=wg, wo=wo, wgu=wgu, wd=wd, nw=nw,
                  slope=slope, idxW=idxW, maskW=maskW, fidx=fidx, kdx=kdx, c256=c256,
                  eps5=eps5, eps6=eps6, neghalf=neghalf, inv2048=inv2048,
                  id16=id16, ones1=ones1, onescol=onescol)
    in_maps = []
    per = 16 // n_cores
    for c in range(n_cores):
        m = dict(common)
        m["x"] = np.ascontiguousarray(x_full[c * per:(c + 1) * per])
        in_maps.append(m)
    return in_maps


_PROG_CACHE = {}


def _get_program():
    if "nc" not in _PROG_CACHE:
        _PROG_CACHE["nc"] = build_core_program(Cfg(n_tiles=8, n_heads=16, debug=False))[0]
    return _PROG_CACHE["nc"]


def kernel(L0_attn_hidden_states, L0_attn_slope_rate, L0_attn_w_qkv, L0_attn_w_gate,
           L0_attn_w_out, L0_attn_norm_w, L0_ffn_w_gate_up, L0_ffn_w_down):
    n_cores = 8
    in_maps = host_prep(L0_attn_hidden_states, L0_attn_slope_rate, L0_attn_w_qkv,
                        L0_attn_w_gate, L0_attn_w_out, L0_attn_norm_w,
                        L0_ffn_w_gate_up, L0_ffn_w_down, n_cores=n_cores)
    nc = _get_program()
    res = run_bass_kernel_spmd(nc, in_maps, core_ids=list(range(n_cores)))
    return np.concatenate([r["out"] for r in res.results], axis=0)
